# version 02
# baseline (speedup 1.0000x reference)
import numpy as np
import ml_dtypes
import bass_rust
import concourse.bass as bass
import concourse.mybir as mybir
from concourse.bass_utils import run_bass_kernel_spmd
from contextlib import ExitStack

F32 = mybir.dt.float32
BF16 = mybir.dt.bfloat16
AF = mybir.ActivationFunctionType
ALU = mybir.AluOpType

ENGINES = ("sync", "scalar", "vector", "tensor", "gpsimd")

RACE_CHECK = True


def _mk_nc():
    return bass.Bass("TRN2", target_bir_lowering=False,
                     detect_race_conditions=RACE_CHECK)

B_PER_CORE = 2
T = 2048
HC = 8
NC = 4
CN = 512


class Prog:

    def __init__(self, nc, ctx):
        self.nc = nc
        self.ctx = ctx
        self.queue = {e: [] for e in ENGINES}
        self.count = {e: 0 for e in ENGINES}
        self.dchan = {}

    def op(self, eng, fn, wait=(), inc=1, chan=None):
        waits = [t for t in wait if t is not None]
        self.queue[eng].append((waits, fn, inc, chan))
        if chan is not None:
            self.dchan[chan] = self.dchan.get(chan, 0) + inc
            return (chan, self.dchan[chan], True)
        self.count[eng] += inc
        return (eng, self.count[eng], False)

    def dma(self, eng, out, in_, chan, wait=(), **kw):
        return self.op(eng, lambda E: E.dma_start(out=out, in_=in_, **kw),
                       wait, inc=16, chan=chan)

    def chan_last(self, chan):
        return (chan, self.dchan.get(chan, 0), True)

    def finalize(self, final_tokens):
        self.op("sync", lambda E: E.nop(), wait=final_tokens)
        sems = {}
        for e in ENGINES:
            sems[(e, False)] = self.ctx.enter_context(self.nc.semaphore(f"sem_{e}"))
        for c in self.dchan:
            sems[(c, True)] = self.ctx.enter_context(self.nc.semaphore(f"dch_{c}"))
        block = self.ctx.enter_context(self.nc.Block())
        for eng in ENGINES:
            ops = self.queue[eng]
            if not ops:
                continue
            dec = getattr(block, eng)

            def body(E, ops=ops, eng=eng):
                lastw = {}
                for waits, fn, inc, chan in ops:
                    for (pkey, val, pdma) in waits:
                        key = (pkey, pdma)
                        if lastw.get(key, 0) >= val:
                            continue
                        E.wait_ge(sems[key], val)
                        lastw[key] = val
                    r = fn(E)
                    if isinstance(r, (list, tuple)):
                        r = r[-1]
                    r.then_inc(sems[(chan, True) if chan is not None else (eng, False)], inc)

            dec(body)


def _after(*tokens):
    return [t for t in tokens if t is not None]



def build_launch_a():
    nc = _mk_nc()
    xT = nc.dram_tensor("xT", [B_PER_CORE, HC, 128, T], F32, kind="ExternalInput").ap()
    WvT = nc.dram_tensor("WvT", [HC, 128, 1024], F32, kind="ExternalInput").ap()
    WrT = nc.dram_tensor("WrT", [HC, 128, 1024], F32, kind="ExternalInput").ap()
    WoT = nc.dram_tensor("WoT", [HC, 128, 1024], F32, kind="ExternalInput").ap()
    vecs = nc.dram_tensor("vecs", [128, 3, 8], F32, kind="ExternalInput").ap()
    xo = nc.dram_tensor("xo", [B_PER_CORE, HC, 128, T], F32, kind="ExternalOutput").ap()

    with ExitStack() as ctx:
        sb = lambda name, shape, dt: ctx.enter_context(nc.sbuf_tensor(name, shape, dt))
        wv = sb("wv", [128, HC, 1024], BF16)
        wr = sb("wr", [128, HC, 1024], BF16)
        wo = sb("wo", [128, HC, 1024], BF16)
        xn = sb("xn", [128, HC, T], BF16)
        rin = sb("rin", [128, HC, T], BF16)
        vres = sb("vres", [128, HC, T], BF16)
        scr16 = sb("scr16", [128, T], BF16)
        ones = sb("ones", [128, 128], BF16)
        vecs_s = sb("vecs_s", [128, 3, 8], F32)
        db = sb("db", [128, 8], F32)
        db2 = sb("db2", [128, 8], F32)
        sm = sb("sm", [128, T], F32)
        sv = sb("sv", [128, T], F32)
        sr = sb("sr", [128, T], F32)
        zb = sb("zb", [128, 2, 2064], BF16)
        xres = sb("xres", [128, T], F32)
        xout = sb("xout", [128, 2, T], F32)
        arena = ctx.enter_context(nc.psum_tensor("arena", [128, 8, CN], F32))

        P = Prog(nc, ctx)

        t_vecs = P.dma("sync", vecs_s[:], vecs, chan="vecs")
        t_ones = P.op("gpsimd", lambda E: E.memset(ones[:], 1.0))
        t_z0 = P.op("gpsimd", lambda E: E.memset(zb[:, :, 0:1], 0.0))
        t_d1 = P.op("scalar", lambda E: E.activation(out=db[:], in_=vecs_s[:, 0, :], func=AF.Exp),
                    wait=[t_vecs])
        t_d2 = P.op("scalar", lambda E: E.activation(out=db2[:], in_=db[:], func=AF.Exp, scale=-1.0),
                    wait=[t_d1])
        for w_s, w_d in ((WvT, wv), (WrT, wr), (WoT, wo)):
            for hc in range(HC):
                P.dma("gpsimd", w_d[:, hc, :], w_s[hc], chan="wload")
        t_wall = P.chan_last("wload")

        def ln_stats(src_chunks, wait_extra, wait_all=()):
            t_mm = [None, None]
            t_sq = None
            for hc in range(HC):
                w0 = _after(wait_extra[hc], t_ones)
                if hc == 0:
                    w0 = w0 + list(wait_all)

                def g1(E, hc=hc, src=src_chunks):
                    r = []
                    for ncc in range(NC):
                        r.append(E.matmul(arena[:, ncc], ones[:],
                                          src(hc)[:, ncc * CN:(ncc + 1) * CN],
                                          start=(hc == 0), stop=(hc == HC - 1)))
                    return r
                t1 = P.op("tensor", g1, wait=w0 + _after(t_mm[0]))
                tq = P.op("scalar",
                          lambda E, hc=hc, src=src_chunks: E.activation(
                              out=scr16[:], in_=src(hc)[:], func=AF.Square),
                          wait=w0 + _after(t_mm[1], t_sq))
                t_sq = tq

                def g2(E, hc=hc):
                    r = []
                    for ncc in range(NC):
                        r.append(E.matmul(arena[:, 4 + ncc], ones[:],
                                          scr16[:, ncc * CN:(ncc + 1) * CN],
                                          start=(hc == 0), stop=(hc == HC - 1)))
                    return r
                t2 = P.op("tensor", g2, wait=[tq])
                t_mm = [t1, t2]
            ps1 = arena[:, 0:4].rearrange("p a b -> p (a b)")
            ps2 = arena[:, 4:8].rearrange("p a b -> p (a b)")
            ta = P.op("vector", lambda E: E.tensor_scalar(out=sm[:], in0=ps1,
                       scalar1=1.0 / 1024, scalar2=None, op0=ALU.mult), wait=[t_mm[0]])
            tb = P.op("vector", lambda E: E.tensor_scalar(out=sv[:], in0=ps2,
                       scalar1=1.0 / 1024, scalar2=None, op0=ALU.mult), wait=[t_mm[1]])
            tc = P.op("vector", lambda E: E.tensor_tensor(out=sr[:], in0=sm[:], in1=sm[:],
                       op=ALU.mult), wait=[ta])
            td = P.op("vector", lambda E: E.tensor_tensor(out=sv[:], in0=sv[:], in1=sr[:],
                       op=ALU.subtract), wait=[tb, tc])
            te = P.op("vector", lambda E: E.tensor_scalar(out=sv[:], in0=sv[:],
                       scalar1=1e-5, scalar2=None, op0=ALU.add), wait=[td])
            tf = P.op("vector", lambda E: E.reciprocal(out=sr[:], in_=sv[:]), wait=[te])
            tg = P.op("scalar", lambda E: E.activation(out=sv[:], in_=sr[:], func=AF.Sqrt),
                      wait=[tf])
            th = P.op("vector", lambda E: E.tensor_tensor(out=sm[:], in0=sm[:], in1=sv[:],
                       op=ALU.mult), wait=[tg])
            return th, tg, (t_mm, t_sq)

        prev_batch_last = {hc: None for hc in range(HC)}
        t_out_all = []

        for b in range(B_PER_CORE):
            for hc in range(HC):
                P.dma("gpsimd", xn[:, hc, :], xT[b, hc],
                      wait=_after(prev_batch_last[hc]), chan=f"cast{b}")
            t_cast_all = P.chan_last(f"cast{b}")

            th, tg, _ = ln_stats(lambda hc: xn[:, hc, :],
                                 {hc: t_cast_all for hc in range(HC)})

            t_norm = {}
            for hc in range(HC):
                t1 = P.op("vector", lambda E, hc=hc: E.tensor_tensor(
                    out=scr16[:], in0=xn[:, hc, :], in1=sv[:], op=ALU.mult),
                    wait=[tg, th])
                t_norm[hc] = P.op("vector", lambda E, hc=hc: E.tensor_tensor(
                    out=xn[:, hc, :], in0=scr16[:], in1=sm[:], op=ALU.subtract),
                    wait=[t1])

            t_mix = {}
            for hc in range(HC):
                pp = hc % 2
                dbc = db2[:, hc:hc + 1].broadcast_to((128, T))
                ts_ = P.op("vector", lambda E, hc=hc, pp=pp, dbc=dbc: E.tensor_tensor_scan(
                    out=zb[:, pp, 1:T + 1], data0=dbc, data1=xn[:, hc, :],
                    initial=0.0, op0=ALU.mult, op1=ALU.add),
                    wait=[t_norm[hc], t_d2, t_z0])
                prev = zb[:, pp, 0:T]
                t1 = P.op("vector", lambda E, hc=hc, prev=prev: E.tensor_tensor(
                    out=scr16[:], in0=xn[:, hc, :], in1=prev, op=ALU.subtract),
                    wait=[ts_])
                t2 = P.op("vector", lambda E, hc=hc, prev=prev: E.scalar_tensor_tensor(
                    out=xn[:, hc, :], in0=scr16[:], scalar=vecs_s[:, 1, hc:hc + 1],
                    in1=prev, op0=ALU.mult, op1=ALU.add), wait=[t1])
                t3 = P.op("vector", lambda E, hc=hc, prev=prev: E.scalar_tensor_tensor(
                    out=rin[:, hc, :], in0=scr16[:], scalar=vecs_s[:, 2, hc:hc + 1],
                    in1=prev, op0=ALU.mult, op1=ALU.add), wait=[t2])
                t_mix[hc] = t3
            t_mix_all = t_mix[HC - 1]

            t_vev = {}
            for oc in range(HC):
                for ncc in range(NC):
                    g = oc * NC + ncc
                    bank = g % 4

                    def gv(E, oc=oc, ncc=ncc, bank=bank):
                        r = []
                        for hc in range(HC):
                            r.append(E.matmul(
                                arena[:, bank], wv[:, hc, oc * 128:(oc + 1) * 128],
                                xn[:, hc, ncc * CN:(ncc + 1) * CN],
                                start=(hc == 0), stop=(hc == HC - 1)))
                        return r
                    tv = P.op("tensor", gv,
                              wait=[t_mix_all, t_wall] + _after(t_vev.get(g - 4)))
                    t_vev[g] = P.op("vector", lambda E, oc=oc, ncc=ncc, bank=bank:
                                    E.tensor_copy(vres[:, oc, ncc * CN:(ncc + 1) * CN],
                                                  arena[:, bank]), wait=[tv])
            t_v_all = t_vev[HC * NC - 4:] if False else [t_vev[g] for g in range(28, 32)]

            t_sig = {}
            t_yev = {}
            for oc in range(HC):
                for ncc in range(NC):
                    g = oc * NC + ncc
                    b1 = 4 + (g % 2)
                    b2 = 6 + (g % 2)

                    def gr(E, oc=oc, ncc=ncc, b1=b1):
                        r = []
                        for hc in range(HC):
                            r.append(E.matmul(
                                arena[:, b1], wr[:, hc, oc * 128:(oc + 1) * 128],
                                rin[:, hc, ncc * CN:(ncc + 1) * CN],
                                start=(hc == 0), stop=(hc == HC - 1)))
                        return r
                    tr = P.op("tensor", gr,
                              wait=[t_mix_all, t_wall] + _after(t_sig.get(g - 2)))
                    t_sig[g] = P.op("scalar", lambda E, g=g, b1=b1: E.activation(
                        out=scr16[:, (g % 2) * CN:(g % 2) * CN + CN],
                        in_=arena[:, b1], func=AF.Sigmoid),
                        wait=[tr] + _after(t_yev.get(g - 2)))

                    def go(E, oc=oc, ncc=ncc, b2=b2):
                        r = []
                        for hc in range(HC):
                            r.append(E.matmul(
                                arena[:, b2], wo[:, hc, oc * 128:(oc + 1) * 128],
                                vres[:, hc, ncc * CN:(ncc + 1) * CN],
                                start=(hc == 0), stop=(hc == HC - 1)))
                        return r
                    to = P.op("tensor", go,
                              wait=t_v_all + [t_wall] + _after(t_yev.get(g - 2)))
                    t_yev[g] = P.op("vector", lambda E, oc=oc, ncc=ncc, g=g, b2=b2:
                                    E.tensor_tensor(
                                        out=xn[:, oc, ncc * CN:(ncc + 1) * CN],
                                        in0=arena[:, b2],
                                        in1=scr16[:, (g % 2) * CN:(g % 2) * CN + CN],
                                        op=ALU.mult),
                                    wait=[to, t_sig[g]])

            th2, tg2, _ = ln_stats(lambda hc: xn[:, hc, :],
                                   {hc: t_yev[hc * NC + NC - 1] for hc in range(HC)},
                                   wait_all=[t_yev[HC * NC - 1], t_sig[HC * NC - 1]])

            t_xr_prev = None
            t_st = {}
            for hc in range(HC):
                pp = hc % 2
                t_xr = P.dma("sync", xres[:], xT[b, hc],
                             wait=_after(t_xr_prev), chan="xres")
                t1 = P.op("vector", lambda E, hc=hc, pp=pp: E.tensor_tensor(
                    out=xout[:, pp, :], in0=xn[:, hc, :], in1=sv[:], op=ALU.mult),
                    wait=[tg2, th2, t_st.get(hc - 2)])
                t2 = P.op("vector", lambda E, pp=pp: E.tensor_tensor(
                    out=xout[:, pp, :], in0=xout[:, pp, :], in1=sm[:], op=ALU.subtract),
                    wait=[t1])
                t3 = P.op("vector", lambda E, pp=pp: E.tensor_tensor(
                    out=xout[:, pp, :], in0=xout[:, pp, :], in1=xres[:], op=ALU.add),
                    wait=[t2, t_xr])
                t_xr_prev = t3
                t_st[hc] = P.dma("sync", xo[b, hc], xout[:, pp, :], wait=[t3],
                                 chan="st")
                prev_batch_last[hc] = t1
            t_out_all.extend(t_st.values())

        P.finalize(t_out_all)
    return nc



NICH = 28
ICG = 4
NE = 8
TT_ = 4096
NTCB = 8


def _build_moe(NICH, SILU_FIRST):
    MRAW = max(NICH * CN, 13312)
    nc = _mk_nc()
    xT = nc.dram_tensor("xT", [B_PER_CORE, HC, 128, T], F32, kind="ExternalInput").ap()
    w1s = nc.dram_tensor("w1s", [NE, NICH, 128, HC, 128], F32, kind="ExternalInput").ap()
    w3s = nc.dram_tensor("w3s", [NE, NICH, 128, HC, 128], F32, kind="ExternalInput").ap()
    w2s = nc.dram_tensor("w2s", [NE, HC, 128, NICH, 128], F32, kind="ExternalInput").ap()
    gwT = nc.dram_tensor("gwT", [HC, 128, NE], F32, kind="ExternalInput").ap()
    rmsw_d = nc.dram_tensor("rmsw", [128, HC], F32, kind="ExternalInput").ap()
    xo = nc.dram_tensor("xo", [B_PER_CORE, HC, 128, T], F32, kind="ExternalOutput").ap()
    cw_dram = nc.dram_tensor("cw_dram", [NE, TT_], F32, kind="Internal")

    with ExitStack() as ctx:
        sb = lambda name, shape, dt: ctx.enter_context(nc.sbuf_tensor(name, shape, dt))
        xn1 = sb("xn1", [128, HC, TT_], BF16)
        m_raw = sb("m_raw", [128, MRAW], BF16)
        h = sb("h", [128, HC, CN], BF16)
        acc = sb("acc", [128, HC, CN], F32)
        wub = sb("wub", [128, 2, ICG, HC, 128], BF16)
        wgb = sb("wgb", [128, 2, ICG, HC, 128], BF16)
        w2b = sb("w2b", [128, NICH, 128], BF16)
        gws = sb("gws", [128, HC, NE], F32)
        rmsw = sb("rmsw_s", [128, HC], F32)
        csum = sb("csum", [1, NE], F32)
        scr16 = sb("scr16", [128, T], BF16)
        dscr = sb("dscr", [128, CN], F32)
        cwbc_t = sb("cwbc_t", [128, CN], F32)
        ones = sb("ones", [128, 128], BF16)
        onesf = sb("onesf", [1, 128], F32)
        onescol = sb("onescol", [128, 1], F32)
        identf = sb("identf", [128, 128], F32)
        sm = sb("sm", [128, T], F32)
        sv = sb("sv", [128, T], F32)
        sr = sb("sr", [128, T], F32)
        rs2p = sb("rs2p", [128, TT_], BF16)
        lgs = sb("lgs", [8, CN], F32)
        cw_row = sb("cw_row", [1, CN], F32)
        cwT_sb = sb("cwT_sb", [8, 128], F32)
        ltile = sb("ltile", [128, 8], F32)
        lmask = sb("lmask", [128, 8], F32)
        cwt = sb("cwt", [128, 8], F32)
        m1c = sb("m1c", [128, 1], F32)
        nm1c = sb("nm1c", [128, 1], F32)
        m2c = sb("m2c", [128, 1], F32)
        e2c = sb("e2c", [128, 1], F32)
        rcc = sb("rcc", [128, 1], F32)
        arena = ctx.enter_context(nc.psum_tensor("arena", [128, 8, CN], F32))

        sx = m_raw[:, 8192:12288].bitcast(F32)
        xch = m_raw[:, 12288:13312].bitcast(F32)
        cwbc = cwbc_t[:]
        xout = m_raw[:, 0:4096].bitcast(F32)
        xres = m_raw[:, 4096:8192].bitcast(F32)

        P = Prog(nc, ctx)

        t_ones = P.op("gpsimd", lambda E: E.memset(ones[:], 1.0))
        t_onesf = P.op("gpsimd", lambda E: E.memset(onesf[:], 1.0))
        t_oc = P.op("gpsimd", lambda E: E.memset(onescol[:], 1.0))
        tid0 = P.op("gpsimd", lambda E: E.memset(identf[:], 0.0))
        tid1 = P.op("gpsimd", lambda E: E.affine_select(
            out=identf[:], in_=identf[:], compare_op=ALU.not_equal, fill=1.0,
            base=0, pattern=[[-1, 128]], channel_multiplier=1), wait=[tid0])
        for hc in range(HC):
            P.dma("gpsimd", gws[:, hc, :], gwT[hc], chan="gw")
        t_gw_all = P.chan_last("gw")
        t_rw = P.dma("sync", rmsw[:], rmsw_d, chan="rmsw")
        t_gws = []
        for hc in range(HC):
            t_gws.append(P.op("vector", lambda E, hc=hc: E.tensor_scalar(
                out=gws[:, hc, :], in0=gws[:, hc, :], scalar1=rmsw[:, hc:hc + 1],
                scalar2=None, op0=ALU.mult), wait=[t_gw_all, t_rw]))

        def g_cs(E):
            r = []
            for hc in range(HC):
                r.append(E.matmul(arena[:, 7][0:1, 0:NE], onescol[:], gws[:, hc, :],
                                  start=(hc == 0), stop=(hc == HC - 1)))
            return r
        t_csmm = P.op("tensor", g_cs, wait=t_gws + [t_oc])
        t_csn = P.op("vector", lambda E: E.tensor_scalar(
            out=csum[:], in0=arena[:, 7][0:1, 0:NE], scalar1=-1.0, scalar2=None,
            op0=ALU.mult), wait=[t_csmm])

        def stats_mms(src_chunks, wait_extra, wait_all=()):
            t_mm = [None, None]
            t_sq = None
            for hc in range(HC):
                w0 = _after(wait_extra[hc], t_ones)
                if hc == 0:
                    w0 = w0 + list(wait_all)

                def g1(E, hc=hc):
                    r = []
                    for ncc in range(NC):
                        r.append(E.matmul(arena[:, ncc], ones[:],
                                          src_chunks(hc)[:, ncc * CN:(ncc + 1) * CN],
                                          start=(hc == 0), stop=(hc == HC - 1)))
                    return r
                t1 = P.op("tensor", g1, wait=w0 + _after(t_mm[0]))
                tq = P.op("scalar",
                          lambda E, hc=hc: E.activation(
                              out=scr16[:], in_=src_chunks(hc)[:], func=AF.Square),
                          wait=w0 + _after(t_mm[1], t_sq))
                t_sq = tq

                def g2(E, hc=hc):
                    r = []
                    for ncc in range(NC):
                        r.append(E.matmul(arena[:, 4 + ncc], ones[:],
                                          scr16[:, ncc * CN:(ncc + 1) * CN],
                                          start=(hc == 0), stop=(hc == HC - 1)))
                    return r
                t2 = P.op("tensor", g2, wait=[tq])
                t_mm = [t1, t2]
            return t_mm

        ps1 = arena[:, 0:4].rearrange("p a b -> p (a b)")
        ps2 = arena[:, 4:8].rearrange("p a b -> p (a b)")

        t_norm = {}
        t_cw_dma = {}
        t_stats_free = None
        t_lg_free = None
        t_tp_free = None
        t_rs2 = {}

        for b in range(B_PER_CORE):
            for hc in range(HC):
                P.dma("gpsimd", xn1[:, hc, b * T:(b + 1) * T], xT[b, hc],
                      chan=f"cast{b}")
            t_cast_all = P.chan_last(f"cast{b}")
            wall = _after(t_stats_free, t_csn)
            if b == 1:
                wall = wall + [t_norm[(0, HC - 1)]]
            t_mm = stats_mms(lambda hc, b=b: xn1[:, hc, b * T:(b + 1) * T],
                             {hc: t_cast_all for hc in range(HC)}, wait_all=wall)
            ta = P.op("vector", lambda E: E.tensor_scalar(out=sm[:], in0=ps1,
                       scalar1=1.0 / 1024, scalar2=None, op0=ALU.mult), wait=[t_mm[0]])
            tb = P.op("vector", lambda E: E.tensor_scalar(out=sv[:], in0=ps2,
                       scalar1=1.0 / 1024, scalar2=None, op0=ALU.mult), wait=[t_mm[1]])
            tc = P.op("vector", lambda E: E.tensor_tensor(out=sx, in0=sm[:], in1=sm[:],
                       op=ALU.mult), wait=[ta])
            td = P.op("vector", lambda E: E.tensor_tensor(out=sv[:], in0=sv[:], in1=sx,
                       op=ALU.subtract), wait=[tb, tc])
            te = P.op("vector", lambda E: E.tensor_scalar(out=sr[:], in0=sv[:],
                       scalar1=1e-5, scalar2=None, op0=ALU.add), wait=[td])
            tf = P.op("vector", lambda E: E.reciprocal(out=sx, in_=sr[:]), wait=[te])
            tg = P.op("scalar", lambda E: E.activation(out=sr[:], in_=sx, func=AF.Sqrt),
                      wait=[tf])
            th = P.op("vector", lambda E: E.tensor_tensor(out=sx, in0=sv[:], in1=sx,
                       op=ALU.mult), wait=[tf, tg])
            ti = P.op("vector", lambda E: E.tensor_scalar(out=sx, in0=sx,
                       scalar1=1e-6, scalar2=None, op0=ALU.add), wait=[th])
            tj = P.op("vector", lambda E: E.reciprocal(out=sv[:], in_=sx), wait=[ti])
            tk = P.op("scalar", lambda E, b=b: E.activation(
                out=rs2p[:, b * T:(b + 1) * T], in_=sv[:], func=AF.Sqrt), wait=[tj])
            t_rs2[b] = tk
            t_stats_free = tb

            for hc in range(HC):
                n1 = P.op("vector", lambda E, hc=hc, b=b: E.tensor_tensor(
                    out=scr16[:], in0=xn1[:, hc, b * T:(b + 1) * T], in1=sm[:],
                    op=ALU.subtract), wait=[ta, t_mm[0], t_mm[1]])
                t_norm[(b, hc)] = P.op("vector", lambda E, hc=hc, b=b: E.tensor_tensor(
                    out=xn1[:, hc, b * T:(b + 1) * T], in0=scr16[:], in1=sr[:],
                    op=ALU.mult), wait=[n1, tg])

            for ncc in range(NC):
                t_prev_mm = None
                for hc in range(HC):
                    t_x = P.dma("sync", xch, xT[b, hc][:, ncc * CN:(ncc + 1) * CN],
                                wait=_after(t_prev_mm), chan="xch")
                    t_prev_mm = P.op("tensor", lambda E, hc=hc: E.matmul(
                        arena[:, 7][0:NE, :], gws[:, hc, :], xch,
                        start=(hc == 0), stop=False),
                        wait=[t_x, t_csn] +
                             ([tb] + _after(t_lg_free) if hc == 0 else []))
                t_r1 = P.op("tensor", lambda E, ncc=ncc: E.matmul(
                    arena[:, 7][0:NE, :], csum[:], sm[0:1, ncc * CN:(ncc + 1) * CN],
                    start=False, stop=True), wait=[ta, t_prev_mm])
                t_s1 = P.op("vector", lambda E, ncc=ncc: E.tensor_tensor(
                    out=lgs[:], in0=arena[:, 7][0:NE, :],
                    in1=sr[0:NE, ncc * CN:(ncc + 1) * CN], op=ALU.mult),
                    wait=[t_r1, tg])
                t_lg_free = t_s1
                t_s2 = P.op("vector", lambda E, ncc=ncc, b=b: E.tensor_tensor(
                    out=lgs[:], in0=lgs[:],
                    in1=rs2p[0:NE, b * T + ncc * CN:b * T + ncc * CN + CN],
                    op=ALU.mult), wait=[t_s1, tk])
                for st in range(CN // 128):
                    tt_ = P.op("tensor", lambda E, st=st: E.transpose(
                        arena[:, 6][0:128, 0:8],
                        lgs[:, st * 128:(st + 1) * 128], identf[0:8, 0:8]),
                        wait=[t_s2, tid1] + _after(t_tp_free))
                    v1 = P.op("vector", lambda E: E.tensor_copy(
                        ltile[:], arena[:, 6][0:128, 0:8]), wait=[tt_])
                    t_tp_free = v1
                    v2 = P.op("vector", lambda E: E.reduce_max(m1c[:], ltile[:],
                              axis=mybir.AxisListType.X), wait=[v1])
                    v3 = P.op("vector", lambda E: E.tensor_scalar(out=nm1c[:], in0=m1c[:],
                              scalar1=-1.0, scalar2=None, op0=ALU.mult), wait=[v2])
                    v4 = P.op("vector", lambda E: E.tensor_scalar(out=lmask[:], in0=ltile[:],
                              scalar1=m1c[:], scalar2=None, op0=ALU.is_equal), wait=[v2])
                    v5 = P.op("vector", lambda E: E.scalar_tensor_tensor(
                        out=lmask[:], in0=lmask[:], scalar=-1e30, in1=ltile[:],
                        op0=ALU.mult, op1=ALU.add), wait=[v4])
                    v6 = P.op("vector", lambda E: E.reduce_max(m2c[:], lmask[:],
                              axis=mybir.AxisListType.X), wait=[v5])
                    v7 = P.op("vector", lambda E: E.tensor_scalar(out=lmask[:], in0=ltile[:],
                              scalar1=m2c[:], scalar2=None, op0=ALU.is_ge), wait=[v6])
                    s1_ = P.op("scalar", lambda E: E.activation(out=e2c[:], in_=m2c[:],
                               func=AF.Exp, bias=nm1c[:]), wait=[v6, v3])
                    v8 = P.op("vector", lambda E: E.tensor_scalar(out=e2c[:], in0=e2c[:],
                              scalar1=1.0, scalar2=None, op0=ALU.add), wait=[s1_])
                    v9 = P.op("vector", lambda E: E.reciprocal(out=rcc[:], in_=e2c[:]),
                              wait=[v8])
                    s2_ = P.op("scalar", lambda E: E.activation(out=cwt[:], in_=ltile[:],
                               func=AF.Exp, bias=nm1c[:]), wait=[v3, v7])
                    v10 = P.op("vector", lambda E: E.tensor_tensor(out=cwt[:], in0=cwt[:],
                               in1=lmask[:], op=ALU.mult), wait=[s2_, v7])
                    v11 = P.op("vector", lambda E: E.tensor_scalar(out=cwt[:], in0=cwt[:],
                               scalar1=rcc[:], scalar2=None, op0=ALU.mult), wait=[v10, v9])
                    tt2 = P.op("tensor", lambda E: E.transpose(
                        arena[:, 5][0:8, 0:128], cwt[:], identf[:]),
                        wait=[v11] + _after(t_cw_dma.get("prev")))
                    v12 = P.op("vector", lambda E: E.tensor_copy(
                        cwT_sb[:], arena[:, 5][0:8, 0:128]), wait=[tt2])
                    P.dma(
                        "sync", cw_dram.ap()[:, b * T + ncc * CN + st * 128:
                                             b * T + ncc * CN + st * 128 + 128],
                        cwT_sb[:], wait=[v12], chan="cwtile")
                    t_cw_dma["prev"] = v12

        t_hb = {}
        t_a_last_pp = {0: None, 1: None}
        t_up_last = None
        t_m_last_reader = None
        t_sa_last = None
        t_bankfree_ag = {0: None, 1: None, 2: None, 3: None}
        t_bankfree_d = {4: None, 5: None}
        t_cwmm_prev = None
        t_cwev_prev = None
        t_accev = {}
        t_y = {}
        t_dlast_e = None
        t_w2_reader = None

        for tcb in range(NTCB):
            b = tcb // (NTCB // 2)
            toff = tcb * CN
            for hc in range(HC):
                b1_ = P.op("vector", lambda E, hc=hc, toff=toff: E.tensor_tensor(
                    out=scr16[:, 0:CN], in0=xn1[:, hc, toff:toff + CN],
                    in1=rs2p[:, toff:toff + CN], op=ALU.mult),
                    wait=[t_norm[(b, hc)], t_rs2[b]] + _after(t_up_last))
                t_hb[(tcb, hc)] = P.op("vector", lambda E, hc=hc: E.tensor_scalar(
                    out=h[:, hc, :], in0=scr16[:, 0:CN], scalar1=rmsw[:, hc:hc + 1],
                    scalar2=None, op0=ALU.mult), wait=[b1_, t_rw])
            t_h_all = t_hb[(tcb, HC - 1)]

            for e in range(NE):
                t_cwr = P.dma("sync", cw_row[:],
                              cw_dram.ap()[e:e + 1, toff:toff + CN],
                              wait=[P.chan_last("cwtile")] + _after(t_cwmm_prev),
                              chan="cwrow")
                t_cwmm = P.op("tensor", lambda E: E.matmul(
                    arena[:, 6], onesf[:], cw_row[:], start=True, stop=True),
                    wait=[t_cwr, t_onesf, t_h_all] + _after(t_cwev_prev))
                t_cwmm_prev = t_cwmm
                t_cwev = P.op("vector", lambda E: E.tensor_copy(cwbc, arena[:, 6]),
                              wait=[t_cwmm] + _after(t_accev.get((e - 1, HC - 1))))
                t_cwev_prev = t_cwev

                t_mdone = {}
                for icg in range(NICH // ICG):
                    pp = icg % 2
                    def wsrc(ws, e=e, icg=icg):
                        off = e * (NICH * 128 * HC * 128) + icg * ICG * (128 * HC * 128)
                        return bass_rust.AP(ws.tensor, off,
                                            [[HC * 128, 128],
                                             [128 * HC * 128, ICG],
                                             [128, HC],
                                             [1, 128]])
                    t_wu = P.dma(
                        "gpsimd", wub[:, pp],
                        wsrc(w1s),
                        wait=_after(t_a_last_pp[pp]), chan=f"wu{pp}")
                    t_wg = P.dma(
                        "gpsimd", wgb[:, pp],
                        wsrc(w3s),
                        wait=_after(t_a_last_pp[pp]), chan=f"wg{pp}")
                    for iq in range(ICG):
                        ich = icg * ICG + iq
                        ba = ich % 2
                        bg = 2 + ich % 2

                        def ga(E, pp=pp, iq=iq, ba=ba):
                            r = []
                            for hc in range(HC):
                                r.append(E.matmul(arena[:, ba], wub[:, pp, iq, hc, :],
                                                  h[:, hc, :], start=(hc == 0),
                                                  stop=(hc == HC - 1)))
                            return r
                        t_a = P.op("tensor", ga, wait=[t_wu, t_h_all] +
                                   _after(t_bankfree_ag[ba]))

                        def gg(E, pp=pp, iq=iq, bg=bg):
                            r = []
                            for hc in range(HC):
                                r.append(E.matmul(arena[:, bg], wgb[:, pp, iq, hc, :],
                                                  h[:, hc, :], start=(hc == 0),
                                                  stop=(hc == HC - 1)))
                            return r
                        t_g = P.op("tensor", gg, wait=[t_wg, t_h_all] +
                                   _after(t_bankfree_ag[bg]))
                        t_a_last_pp[pp] = t_g
                        t_up_last = t_g
                        b_silu = ba if SILU_FIRST else bg
                        b_lin = bg if SILU_FIRST else ba
                        t_silu_src = t_a if SILU_FIRST else t_g
                        t_lin_src = t_g if SILU_FIRST else t_a
                        t_sa = P.op("scalar", lambda E, b_silu=b_silu, iq=iq: E.activation(
                            out=scr16[:, CN + (iq % 2) * CN:CN + (iq % 2) * CN + CN],
                            in_=arena[:, b_silu], func=AF.Silu),
                            wait=[t_silu_src] + _after(t_mdone.get(ich - 2)))
                        t_sa_last = t_sa
                        t_bankfree_ag[b_silu] = t_sa
                        t_m = P.op("vector", lambda E, ich=ich, b_lin=b_lin, iq=iq:
                                   E.tensor_tensor(
                                       out=m_raw[:, ich * CN:(ich + 1) * CN],
                                       in0=arena[:, b_lin],
                                       in1=scr16[:, CN + (iq % 2) * CN:CN + (iq % 2) * CN + CN],
                                       op=ALU.mult),
                                   wait=[t_lin_src, t_sa] + _after(t_m_last_reader if ich < 2 else None))
                        t_mdone[ich] = t_m
                        t_bankfree_ag[b_lin] = t_m
                t_m_all = t_mdone[NICH - 1]

                for oc in range(HC):
                    bd = 4 + oc % 2
                    t_w2 = P.dma(
                        "gpsimd", w2b[:].rearrange("p a b -> p (a b)"),
                        w2s[e, oc].rearrange("p a b -> p (a b)"),
                        wait=_after(t_w2_reader), chan="w2")

                    def gd(E, bd=bd):
                        r = []
                        for ich in range(NICH):
                            r.append(E.matmul(arena[:, bd], w2b[:, ich, :],
                                              m_raw[:, ich * CN:(ich + 1) * CN],
                                              start=(ich == 0), stop=(ich == NICH - 1)))
                        return r
                    t_d = P.op("tensor", gd, wait=[t_w2, t_m_all] +
                               _after(t_bankfree_d[bd]))
                    t_w2_reader = t_d
                    t_m_last_reader = t_d
                    t_dlast_e = t_d
                    if e == 0:
                        tacc = P.op("vector", lambda E, oc=oc, bd=bd: E.tensor_tensor(
                            out=acc[:, oc, :], in0=arena[:, bd], in1=cwbc,
                            op=ALU.mult), wait=[t_d, t_cwev])
                        t_accev[(e, oc)] = tacc
                        t_bankfree_d[bd] = tacc
                    else:
                        d1 = P.op("vector", lambda E, bd=bd: E.tensor_tensor(
                            out=dscr[:], in0=arena[:, bd], in1=cwbc, op=ALU.mult),
                            wait=[t_d, t_cwev])
                        t_bankfree_d[bd] = d1
                        tacc = P.op("vector", lambda E, oc=oc: E.tensor_tensor(
                            out=acc[:, oc, :], in0=acc[:, oc, :], in1=dscr[:],
                            op=ALU.add), wait=[d1])
                        t_accev[(e, oc)] = tacc
                        if e == NE - 1:
                            t_y[(tcb, oc)] = P.op(
                                "vector", lambda E, oc=oc, toff=toff: E.tensor_tensor(
                                    out=xn1[:, oc, toff:toff + CN], in0=acc[:, oc, :],
                                    in1=xn1[:, oc, toff:toff + CN], op=ALU.add),
                                wait=[tacc])

        t_out_all = []
        t_res_prev = None
        t_post_g2 = None
        for b in range(B_PER_CORE):
            wait_y = {hc: t_y[(b * (NTCB // 2) + NTCB // 2 - 1, hc)] for hc in range(HC)}
            wall = [t_y[(NTCB - 1, HC - 1)], t_sa_last] + _after(t_res_prev, t_post_g2)
            t_mm = stats_mms(lambda hc, b=b: xn1[:, hc, b * T:(b + 1) * T],
                             wait_y, wait_all=wall)
            t_post_g2 = t_mm[1]
            ta = P.op("vector", lambda E: E.tensor_scalar(out=sm[:], in0=ps1,
                       scalar1=1.0 / 1024, scalar2=None, op0=ALU.mult), wait=[t_mm[0]])
            tb = P.op("vector", lambda E: E.tensor_scalar(out=sv[:], in0=ps2,
                       scalar1=1.0 / 1024, scalar2=None, op0=ALU.mult), wait=[t_mm[1]])
            tc = P.op("vector", lambda E: E.tensor_tensor(out=sr[:], in0=sm[:], in1=sm[:],
                       op=ALU.mult), wait=[ta])
            td = P.op("vector", lambda E: E.tensor_tensor(out=sv[:], in0=sv[:], in1=sr[:],
                       op=ALU.subtract), wait=[tb, tc])
            te = P.op("vector", lambda E: E.tensor_scalar(out=sv[:], in0=sv[:],
                       scalar1=1e-5, scalar2=None, op0=ALU.add), wait=[td])
            tf = P.op("vector", lambda E: E.reciprocal(out=sr[:], in_=sv[:]), wait=[te])
            tg = P.op("scalar", lambda E: E.activation(out=sv[:], in_=sr[:], func=AF.Sqrt),
                      wait=[tf])
            th = P.op("vector", lambda E: E.tensor_tensor(out=sm[:], in0=sm[:], in1=sv[:],
                       op=ALU.mult), wait=[tg])
            t_xr_prev = t_y[(NTCB - 1, HC - 1)] if b == 0 else None
            t_st = {}
            for hc in range(HC):
                t_xr = P.dma("sync", xres, xT[b, hc], wait=_after(t_xr_prev),
                             chan="xres")
                t1 = P.op("vector", lambda E, hc=hc, b=b: E.tensor_tensor(
                    out=xout, in0=xn1[:, hc, b * T:(b + 1) * T], in1=sv[:], op=ALU.mult),
                    wait=[tg, th, t_st.get(hc - 1)])
                t2 = P.op("vector", lambda E: E.tensor_tensor(
                    out=xout, in0=xout, in1=sm[:], op=ALU.subtract), wait=[t1])
                t3 = P.op("vector", lambda E: E.tensor_tensor(
                    out=xout, in0=xout, in1=xres, op=ALU.add), wait=[t2, t_xr])
                t_xr_prev = t3
                t_st[hc] = P.dma("sync", xo[b, hc], xout, wait=[t3], chan="st")
            t_res_prev = t_xr_prev
            t_out_all.extend(t_st.values())

        P.finalize(t_out_all)
    return nc




def build_launch_b():
    return _build_moe(28, True)


def build_launch_d():
    return _build_moe(16, False)


def _stage_b(inputs, core, x1_full):
    b0 = core * B_PER_CORE
    xT = np.ascontiguousarray(
        x1_full[b0:b0 + B_PER_CORE].transpose(0, 2, 1)).reshape(B_PER_CORE, HC, 128, T)
    w1 = inputs["L0_ffn_w1"]
    w3 = inputs["L0_ffn_w3"]
    w2 = inputs["L0_ffn_w2"]
    w1s = np.ascontiguousarray(w1.reshape(8, 28, 128, 8, 128).transpose(0, 1, 4, 3, 2))
    w3s = np.ascontiguousarray(w3.reshape(8, 28, 128, 8, 128).transpose(0, 1, 4, 3, 2))
    w2s = np.ascontiguousarray(w2.reshape(8, 8, 128, 28, 128).transpose(0, 1, 4, 3, 2))
    return {
        "xT": xT, "w1s": w1s, "w3s": w3s, "w2s": w2s,
        "gwT": np.ascontiguousarray(inputs["L0_ffn_gate_w"].T).reshape(HC, 128, NE),
        "rmsw": np.ascontiguousarray(inputs["L0_ffn_rms_weight"].reshape(HC, 128).T),
    }


def run_b(inputs, x1_full):
    nc = build_launch_b()
    in_maps = [_stage_b(inputs, c, x1_full) for c in range(8)]
    res = run_bass_kernel_spmd(nc, in_maps, core_ids=list(range(8)))
    return _unstage_x(res.results)


def _stage_d(inputs, core, x3_full):
    b0 = core * B_PER_CORE
    xT = np.ascontiguousarray(
        x3_full[b0:b0 + B_PER_CORE].transpose(0, 2, 1)).reshape(B_PER_CORE, HC, 128, T)
    wu = inputs["L1_ffn_w_up"]
    wd = inputs["L1_ffn_w_down"]
    u_part = wu[:, 0:2048, :]
    g_part = wu[:, 2048:4096, :]
    w1s = np.ascontiguousarray(u_part.reshape(8, 16, 128, 8, 128).transpose(0, 1, 4, 3, 2))
    w3s = np.ascontiguousarray(g_part.reshape(8, 16, 128, 8, 128).transpose(0, 1, 4, 3, 2))
    w2s = np.ascontiguousarray(wd.reshape(8, 8, 128, 16, 128).transpose(0, 1, 4, 3, 2))
    return {
        "xT": xT, "w1s": w1s, "w3s": w3s, "w2s": w2s,
        "gwT": np.ascontiguousarray(inputs["L1_ffn_w_router"].T).reshape(HC, 128, NE),
        "rmsw": np.ascontiguousarray(inputs["L1_ffn_norm_scale"].reshape(HC, 128).T),
    }


def run_d(inputs, x3_full):
    nc = build_launch_d()
    in_maps = [_stage_d(inputs, c, x3_full) for c in range(8)]
    res = run_bass_kernel_spmd(nc, in_maps, core_ids=list(range(8)))
    return _unstage_x(res.results)



NH = 16
CSTG = 4
HD = 64
NB = 63
CEXP = 16.0
WPAD = 576
NT = 16


def build_launch_c():
    nc = _mk_nc()
    xT = nc.dram_tensor("xT", [B_PER_CORE, HC, 128, T], F32, kind="ExternalInput").ap()
    wqs = nc.dram_tensor("wqs", [24, HC, 128, 128], F32, kind="ExternalInput").ap()
    wps = nc.dram_tensor("wps", [HC, 128, 1024], F32, kind="ExternalInput").ap()
    bpv = nc.dram_tensor("bpv", [128, HC], F32, kind="ExternalInput").ap()
    rpeq = nc.dram_tensor("rpeq", [NH, HD, NB], F32, kind="ExternalInput").ap()
    rpek = nc.dram_tensor("rpek", [NH, HD, NB], F32, kind="ExternalInput").ap()
    rpev = nc.dram_tensor("rpev", [NH, NB, HD], F32, kind="ExternalInput").ap()
    xo = nc.dram_tensor("xo", [B_PER_CORE, HC, 128, T], F32, kind="ExternalOutput").ap()

    qkv_dram = nc.dram_tensor("qkv_dram", [24 * 128, T], BF16, kind="Internal")
    qle_dram = nc.dram_tensor("qle_dram", [T, WPAD], BF16, kind="Internal")
    ekle_dram = nc.dram_tensor("ekle_dram", [T, WPAD], BF16, kind="Internal")
    slab_dram = nc.dram_tensor("slab_dram", [128, 392], BF16, kind="Internal")
    xout_dram = nc.dram_tensor("xout_dram", [1024, T], BF16, kind="Internal")

    with ExitStack() as ctx:
        sb = lambda name, shape, dt: ctx.enter_context(nc.sbuf_tensor(name, shape, dt))
        xn = sb("xn", [128, HC, T], BF16)
        qbuf = sb("qbuf", [64, T], BF16)
        kbuf = sb("kbuf", [64, T], BF16)
        vbuf = sb("vbuf", [64, T], BF16)
        probsT = sb("probsT", [128, NT, 1024], BF16)
        probs = sb("probs", [128, T], BF16)
        wqb = sb("wqb", [128, 2, HC, 128], BF16)
        wpb = sb("wpb", [128, HC, 1024], BF16)
        bcol = sb("bcol", [128, HC], F32)
        qtmp = sb("qtmp", [128, T], BF16)
        rqb = sb("rqb", [64, NB], BF16)
        rkb = sb("rkb", [64, NB], BF16)
        qltest = sb("qltest", [64, 64], BF16)
        qltest2 = sb("qltest2", [128, 64], BF16)
        rvband = sb("rvband", [61, HD], BF16)
        rv0f = sb("rv0f", [1, HD], BF16)
        rv62f = sb("rv62f", [1, HD], BF16)
        srowb = sb("srowb", [1, 128], BF16)
        slrowb = sb("slrowb", [1, 128], BF16)
        qlb = sb("qlb", [128, NB], F32)
        qlcols = sb("qlcols", [128, NT, 2], F32)
        ewk = sb("ewk", [128, NT, 64], F32)
        kls = sb("kls", [128, 64], F32)
        pad = sb("pad", [128, WPAD], BF16)
        qse = sb("qse", [128, 384], BF16)
        ekst = sb("ekst", [128, 384], BF16)
        vab = sb("vab", [128, NT, 128], BF16)
        vbb = sb("vbb", [128, NT, 128], BF16)
        m64 = sb("m64", [128, 128], BF16)
        mrm = sb("mrm", [128, 2, 128], BF16)
        mscr = sb("mscr", [128, 128], BF16)
        dsl = sb("dsl", [128, 61], BF16)
        dtb = sb("dtb", [61, 128], BF16)
        srzbuf = sb("srzbuf", [128, T], BF16)
        srow = sb("srow", [1, 128], F32)
        zq = sb("zq", [1, 128], F32)
        zrow = sb("zrow", [1, T], F32)
        selp = sb("selp", [128, 2], BF16)
        zbc = sb("zbc", [64, CN], F32)
        bs = sb("bs", [1, 128], F32)
        slrow = sb("slrow", [1, 128], F32)
        tb64 = sb("tb64", [64, 128], F32)
        outh = sb("outh", [64, T], BF16)
        slab = sb("slab", [128, 392], BF16)
        ones = sb("ones", [128, 128], BF16)
        onesf = sb("onesf", [1, 64], F32)
        oc61 = sb("oc61", [61, 1], BF16)
        ncexp = sb("ncexp", [128, 1], F32)
        identb = sb("identb", [128, 128], BF16)
        scr16 = sb("scr16", [128, T], BF16)
        sm = sb("sm", [128, T], F32)
        sv = sb("sv", [128, T], F32)
        sr = sb("sr", [128, T], F32)
        xres = sb("xres", [128, T], F32)
        xoutb = sb("xoutb", [128, T], F32)
        arena = ctx.enter_context(nc.psum_tensor("arena", [128, 8, CN], F32))

        P = Prog(nc, ctx)
        psA = arena[:, 6][:, 0:128]
        psB = arena[:, 7][:, 0:128]
        psBs = arena[:, 7][:, 128:256]
        ps1 = arena[:, 0:4].rearrange("p a b -> p (a b)")
        ps2 = arena[:, 4:8].rearrange("p a b -> p (a b)")

        t_ones = P.op("gpsimd", lambda E: E.memset(ones[:], 1.0))
        t_onesf = P.op("gpsimd", lambda E: E.memset(onesf[:], 1.0))
        t_o61 = P.op("gpsimd", lambda E: E.memset(oc61[:], 1.0))
        t_nce = P.op("gpsimd", lambda E: E.memset(ncexp[:], -CEXP))
        ti0 = P.op("gpsimd", lambda E: E.memset(identb[:], 0.0))
        ti1 = P.op("gpsimd", lambda E: E.affine_select(
            out=identb[:], in_=identb[:], compare_op=ALU.not_equal, fill=1.0,
            base=0, pattern=[[-1, 128]], channel_multiplier=1), wait=[ti0])
        tm0 = P.op("gpsimd", lambda E: E.memset(mrm[:].rearrange("p a b -> p (a b)"), 1.0))
        t_mask = None
        for ri, r in enumerate((0, 1)):
            t_mask = P.op("gpsimd", lambda E, ri=ri, r=r: E.affine_select(
                out=mrm[:, ri, :], in_=mrm[:, ri, :], compare_op=ALU.is_ge,
                fill=0.0, base=128 * r - 31, pattern=[[-1, 128]],
                channel_multiplier=1), wait=[tm0])
        tv0 = P.op("gpsimd", lambda E: E.memset(vab[:].rearrange("p a b -> p (a b)"), 0.0))
        tv1 = P.op("gpsimd", lambda E: E.memset(vbb[:].rearrange("p a b -> p (a b)"), 0.0))
        tv2 = P.op("gpsimd", lambda E: E.memset(
            bass_rust.AP(vbb.ap().tensor, 64, [[128, 128], [128 * NT // NT * 0 + 128, NT], [1, 2]])
            if False else vbb[:].rearrange("p a b -> p a b")[:, :, 64:66], 1.0), wait=[tv1])
        tv3 = P.op("gpsimd", lambda E: E.memset(vab[:].rearrange("p a b -> p a b")[:, :, 65:66], 1.0),
                   wait=[tv0])
        tsp0 = P.op("gpsimd", lambda E: E.memset(selp[:], 0.0))
        t_padms = P.op("gpsimd", lambda E: E.memset(pad[:], 0.0))
        t_selp = P.op("gpsimd", lambda E: E.affine_select(
            out=selp[:], in_=selp[:], compare_op=ALU.not_equal, fill=1.0,
            base=-64, pattern=[[-1, 2]], channel_multiplier=1), wait=[tsp0])
        tm64a = P.op("gpsimd", lambda E: E.memset(m64[:], 0.0))
        tm64 = P.op("gpsimd", lambda E: E.memset(m64[:, 64:65], 1.0), wait=[tm64a])
        t_vcols = tv3
        for hc in range(HC):
            P.dma("gpsimd", wpb[:, hc, :], wps[hc], chan="wp")
        t_wp_all = P.chan_last("wp")
        t_bp = P.dma("sync", bcol[:], bpv, chan="bp")

        AX = mybir.AxisListType.X

        def stats_mms(src_chunks, wait_extra, wait_all=()):
            t_mm = [None, None]
            t_sq = None
            for hc in range(HC):
                w0 = _after(wait_extra[hc], t_ones)
                if hc == 0:
                    w0 = w0 + list(wait_all)

                def g1(E, hc=hc):
                    return [E.matmul(arena[:, ncc], ones[:],
                                     src_chunks(hc)[:, ncc * CN:(ncc + 1) * CN],
                                     start=(hc == 0), stop=(hc == HC - 1))
                            for ncc in range(NC)]
                t1 = P.op("tensor", g1, wait=w0 + _after(t_mm[0]))
                tq = P.op("scalar", lambda E, hc=hc: E.activation(
                    out=scr16[:], in_=src_chunks(hc)[:], func=AF.Square),
                    wait=w0 + _after(t_mm[1], t_sq))
                t_sq = tq

                def g2(E, hc=hc):
                    return [E.matmul(arena[:, 4 + ncc], ones[:],
                                     scr16[:, ncc * CN:(ncc + 1) * CN],
                                     start=(hc == 0), stop=(hc == HC - 1))
                            for ncc in range(NC)]
                t2 = P.op("tensor", g2, wait=[tq])
                t_mm = [t1, t2]
            return t_mm

        def ln_basic(t_mm):
            ta = P.op("vector", lambda E: E.tensor_scalar(out=sm[:], in0=ps1,
                       scalar1=1.0 / 1024, scalar2=None, op0=ALU.mult), wait=[t_mm[0]])
            tb = P.op("vector", lambda E: E.tensor_scalar(out=sv[:], in0=ps2,
                       scalar1=1.0 / 1024, scalar2=None, op0=ALU.mult), wait=[t_mm[1]])
            tc = P.op("vector", lambda E: E.tensor_tensor(out=sr[:], in0=sm[:], in1=sm[:],
                       op=ALU.mult), wait=[ta])
            td = P.op("vector", lambda E: E.tensor_tensor(out=sv[:], in0=sv[:], in1=sr[:],
                       op=ALU.subtract), wait=[tb, tc])
            te = P.op("vector", lambda E: E.tensor_scalar(out=sv[:], in0=sv[:],
                       scalar1=1e-5, scalar2=None, op0=ALU.add), wait=[td])
            tf = P.op("vector", lambda E: E.reciprocal(out=sr[:], in_=sv[:]), wait=[te])
            tg = P.op("scalar", lambda E: E.activation(out=sv[:], in_=sr[:], func=AF.Sqrt),
                      wait=[tf])
            th = P.op("vector", lambda E: E.tensor_tensor(out=sm[:], in0=sm[:], in1=sv[:],
                       op=ALU.mult), wait=[tg])
            return tg, th

        t_out_all = []
        t_res_prev = None
        t_post_g2 = None
        t_xn_free = {hc: None for hc in range(HC)}
        t_wq_reader = {0: None, 1: None}
        t_qkv_rd_last = None
        t_qle_rd_last = None
        t_ekle_rd_last = None
        t_dread_last = None
        t_slabw_last = None
        t_qk_free = None
        t_v_free = None
        t_rq_free = None
        t_rk_free = None
        t_rv_free = None
        t_outw_rd = None
        t_outh_free = None
        t_probsT_free = None
        t_tb_last = None
        t_dtb_prev = None
        t_qa_last = None
        t_erd_last = None
        t_sw_prev = None
        t_slab_ev_prev = None
        t_vt_ev = None
        t_tpass = None

        for b in range(B_PER_CORE):
            for hc in range(HC):
                P.dma("gpsimd", xn[:, hc, :], xT[b, hc],
                      wait=_after(t_xn_free[hc]), chan=f"cast{b}")
            t_cast_all = P.chan_last(f"cast{b}")
            wall = _after(t_post_g2, t_res_prev)
            t_mm = stats_mms(lambda hc: xn[:, hc, :],
                             {hc: t_cast_all for hc in range(HC)}, wait_all=wall)
            tg, th = ln_basic(t_mm)
            t_norm = {}
            for hc in range(HC):
                n1 = P.op("vector", lambda E, hc=hc: E.tensor_tensor(
                    out=scr16[:], in0=xn[:, hc, :], in1=sv[:], op=ALU.mult),
                    wait=[tg, th, t_mm[0], t_mm[1]])
                t_norm[hc] = P.op("vector", lambda E, hc=hc: E.tensor_tensor(
                    out=xn[:, hc, :], in0=scr16[:], in1=sm[:], op=ALU.subtract),
                    wait=[n1])
            t_norm_all = t_norm[HC - 1]

            t_qet = {}
            for oc3 in range(24):
                pp = oc3 % 2
                t_wq = P.dma("gpsimd", wqb[:, pp],
                             wqs[oc3].rearrange("a p b -> p a b"),
                             wait=_after(t_wq_reader[pp]), chan=f"wq{pp}")

                def gq(E, pp=pp):
                    r = []
                    for ncc in range(NC):
                        for hc in range(HC):
                            r.append(E.matmul(arena[:, ncc], wqb[:, pp, hc, :],
                                              xn[:, hc, ncc * CN:(ncc + 1) * CN],
                                              start=(hc == 0), stop=(hc == HC - 1)))
                    return r
                t_q = P.op("tensor", gq, wait=[t_wq, t_norm_all] +
                           _after(t_qet.get(oc3 - 1)))
                t_wq_reader[pp] = t_q
                if oc3 < 8:
                    tev = P.op("vector", lambda E: E.tensor_scalar(
                        out=qtmp[:], in0=ps1, scalar1=0.125, scalar2=None,
                        op0=ALU.mult), wait=[t_q])
                else:
                    tev = P.op("vector", lambda E: E.tensor_copy(qtmp[:], ps1),
                               wait=[t_q])
                t_qet[oc3] = tev
                P.dma("sync", qkv_dram.ap()[oc3 * 128:(oc3 + 1) * 128, :], qtmp[:],
                      wait=[tev] + _after(t_qkv_rd_last if oc3 == 0 else None),
                      chan="qkvw")
            t_qkv_w_all = P.chan_last("qkvw")

            for h in range(NH):
                def qkv_src(row0, nrows):
                    return bass_rust.AP(qkv_dram.ap().tensor, row0 * T,
                                        [[T, nrows], [1, T]])
                t_q1 = P.dma("gpsimd", qbuf[:], qkv_src(h * 64, 64),
                             wait=[t_qkv_w_all] + _after(t_qk_free), chan="qh")
                t_k1 = P.dma("gpsimd", kbuf[:], qkv_src(1024 + h * 64, 64),
                             wait=[t_qkv_w_all] + _after(t_qk_free), chan="kh")
                t_v1 = P.dma("gpsimd", vbuf[:], qkv_src(2048 + h * 64, 64),
                             wait=[t_qkv_w_all] + _after(t_v_free), chan="vh")
                t_rq = P.dma("gpsimd", rqb[:], rpeq[h], wait=_after(t_rq_free),
                             chan="rq")
                t_rk = P.dma("gpsimd", rkb[:], rpek[h], wait=_after(t_rk_free),
                             chan="rk")
                t_rv = P.dma("gpsimd", rvband[:], rpev[h][1:62, :],
                             wait=_after(t_rv_free), chan="rv")
                t_rv0 = P.dma("gpsimd", rv0f[:], rpev[h][0:1, :],
                              wait=_after(t_rv_free), chan="rv0")
                t_rv62 = P.dma("gpsimd", rv62f[:], rpev[h][62:63, :],
                               wait=_after(t_rv_free), chan="rv62")

                t_ek = {}
                t_ekw = {}
                t_qlcs = {}
                t_qlw = {}
                t_vt = {}
                t_km_last = None
                t_qm_last = None
                t_kv_prev = None
                t_qlcp_prev = None
                t_qlb_rd_prev = None

                def emit_kl(jt, extra_wait=()):
                    nonlocal t_km_last, t_kv_prev
                    t_km = P.op("tensor", lambda E, jt=jt: E.matmul(
                        arena[:, 7][:, 0:NB], kbuf[:, jt * 128:(jt + 1) * 128],
                        rqb[:], start=True, stop=True),
                        wait=[t_k1, t_rq] + _after(t_kv_prev, t_qlcp_prev) +
                             list(extra_wait))
                    t_km_last = t_km
                    t_kv = P.op("vector", lambda E: E.tensor_scalar(
                        out=kls[:, 0:NB], in0=arena[:, 7][:, 0:NB], scalar1=0.125,
                        scalar2=None, op0=ALU.mult),
                        wait=[t_km] + _after(t_ek.get(jt - 1)))
                    t_kv_prev = t_kv
                    t_ek[jt] = P.op("scalar", lambda E, jt=jt: E.activation(
                        out=ewk[:, jt, 0:NB], in_=kls[:, 0:NB], func=AF.Exp),
                        wait=[t_kv])
                    w_extra = (_after(t_ekle_rd_last, t_padms) +
                               [P.chan_last("qlew")]
                               if jt == 0 and h == 0 and b == 0 else [])
                    p1 = P.op("scalar", lambda E, jt=jt: E.activation(
                        out=pad[:, 0:256], in_=pad[:, 0:256], func=AF.Identity,
                        scale=0.0, bias=ewk[:, jt, 0:1]),
                        wait=[t_ek[jt]] + [P.chan_last("eklew"), P.chan_last("qlew")]
                             + w_extra)
                    p2 = P.op("vector", lambda E, jt=jt: E.tensor_copy(
                        pad[:, 256:319], ewk[:, jt, 0:NB]), wait=[t_ek[jt], p1])
                    p3 = P.op("scalar", lambda E, jt=jt: E.activation(
                        out=pad[:, 319:WPAD], in_=pad[:, 319:WPAD], func=AF.Identity,
                        scale=0.0, bias=ewk[:, jt, 62:63]), wait=[t_ek[jt], p1])
                    t_ekw[jt] = P.dma("sync",
                                      ekle_dram.ap()[jt * 128:(jt + 1) * 128, :],
                                      pad[:], wait=[p1, p2, p3, t_ekst_con.get(jt, None)]
                                      if t_ekst_con.get(jt) is not None else [p1, p2, p3],
                                      chan="eklew")

                def emit_ql(it):
                    nonlocal t_qm_last, t_qlcp_prev, t_qlb_rd_prev
                    t_qm = P.op("tensor", lambda E, it=it: E.matmul(
                        arena[:, 7][:, 0:NB], qbuf[:, it * 128:(it + 1) * 128],
                        rkb[:], start=True, stop=True),
                        wait=[t_q1, t_rk] + _after(t_qlcp_prev, t_tb_last,
                                                   t_kv_prev, t_qlb_rd_prev))
                    t_qm_last = t_qm
                    t_qlcp = P.op("vector", lambda E: E.tensor_copy(
                        qlb[:, 0:NB], arena[:, 7][:, 0:NB]), wait=[t_qm])
                    t_qlcp_prev = t_qlcp
                    t_qlcs[it] = P.op("vector", lambda E, it=it: E.tensor_scalar(
                        out=qlcols[:, it, 0:2],
                        in0=bass_rust.AP(qlb.ap().tensor, 0, [[NB, 128], [62, 2]]),
                        scalar1=-CEXP, scalar2=None, op0=ALU.add),
                        wait=[t_qlcp])
                    w_extra = ([P.chan_last("qse")] if True else [])
                    p1 = P.op("scalar", lambda E: E.activation(
                        out=pad[:, 0:256], in_=pad[:, 0:256], func=AF.Identity,
                        scale=0.0, bias=qlb[:, 0:1]),
                        wait=[t_qlcp, P.chan_last("eklew"), P.chan_last("qlew")] +
                             w_extra)
                    p2 = P.op("vector", lambda E: E.tensor_copy(
                        pad[:, 256:319], qlb[:, 0:NB]), wait=[t_qlcp, p1])
                    p3 = P.op("scalar", lambda E: E.activation(
                        out=pad[:, 319:WPAD], in_=pad[:, 319:WPAD], func=AF.Identity,
                        scale=0.0, bias=qlb[:, 62:63]), wait=[t_qlcp, p1])
                    t_qlb_rd_prev = p3
                    t_qlw[it] = P.dma("sync",
                                      qle_dram.ap()[it * 128:(it + 1) * 128, :],
                                      pad[:], wait=[p1, p2, p3], chan="qlew")

                def emit_vtm(jt):
                    nonlocal t_vt_ev, t_vt_mm
                    ttv = P.op("tensor", lambda E, jt=jt: E.transpose(
                        arena[:, 5].bitcast(BF16)[:, 0:64],
                        vbuf[:, jt * 128:(jt + 1) * 128], identb[0:64, 0:64]),
                        wait=[t_v1, ti1, t_vcols] +
                             _after(t_vt_ev, t_slab_ev_prev if jt == 0 else None,
                                    t_tp_ev.get((jt // 2 - 1, 1))))
                    t_vt_mm = ttv
                    c1 = P.op("vector", lambda E, jt=jt: E.tensor_copy(
                        vab[:, jt, 0:64], arena[:, 5].bitcast(BF16)[:, 0:64]),
                        wait=[ttv])
                    t_vt[jt] = P.op("vector", lambda E, jt=jt: E.tensor_copy(
                        vbb[:, jt, 0:64], arena[:, 5].bitcast(BF16)[:, 0:64]),
                        wait=[ttv, c1])
                    t_vt_ev = t_vt[jt]

                t_ekst_con = {}
                t_vt_mm = None
                t_tpass = None
                t_texp = {}
                t_tpmm = {}
                t_tp_ev = {}
                t_qse_add = {}
                t_outev = {}
                t_sc_last = None
                t_srz_cp = {}
                for half in range(2):
                    i_lo = half * 1024
                    for qi in range(8):
                        it = half * 8 + qi
                        I0 = it * 128
                        if half == 0:
                            if 2 * it < NT:
                                emit_vtm(2 * it)
                            if 2 * it + 1 < NT:
                                emit_vtm(2 * it + 1)
                            klist = [2 * it, 2 * it + 1]
                        else:
                            klist = []
                        for jt_ in klist:
                            if jt_ not in t_ek:
                                emit_kl(jt_, extra_wait=_after(t_tb_last))
                        emit_ql(it)

                        def gsc(E, it=it):
                            return [E.matmul(arena[:, ncc],
                                             qbuf[:, it * 128:(it + 1) * 128],
                                             kbuf[:, ncc * CN:(ncc + 1) * CN],
                                             start=True, stop=True)
                                    for ncc in range(NC)]
                        t_sc = P.op("tensor", gsc,
                                    wait=[t_q1, t_k1] + _after(t_texp.get(it - 1)))
                        t_sc_last = t_sc
                        jw0 = max(0, I0 - 128)
                        jw1 = min(T, I0 + 256)
                        Wj = jw1 - jw0
                        qse_ap = bass_rust.AP(
                            qle_dram.ap().tensor,
                            I0 * (WPAD - 1) + 256 + jw0,
                            [[WPAD - 1, 128], [1, Wj]])
                        t_qrd = P.dma("sync", qse[:, 0:Wj], qse_ap,
                                      wait=[t_qlw[it]] +
                                           _after(t_qse_add.get(it - 1),
                                                  t_qa_last if it == 0 else None),
                                      chan="qse")
                        t_qa = P.op("vector", lambda E, jw0=jw0, Wj=Wj: E.tensor_tensor(
                            out=ps1[:, jw0:jw0 + Wj], in0=ps1[:, jw0:jw0 + Wj],
                            in1=qse[:, 0:Wj], op=ALU.add), wait=[t_qrd, t_sc])
                        t_qse_add[it] = t_qa
                        texps = []
                        tw_prev = _after(t_tpmm.get(it - 1))
                        if it >= 2:
                            texps.append(P.op("scalar", lambda E, it=it:
                                E.activation(out=probs[:, 0:(it - 1) * 128],
                                             in_=ps1[:, 0:(it - 1) * 128],
                                             func=AF.Exp, scale=1.0,
                                             bias=qlcols[:, it, 0:1]),
                                wait=[t_qa, t_qlcs[it]] + tw_prev))
                        texps.append(P.op("scalar", lambda E, jw0=jw0, jw1=jw1:
                            E.activation(out=probs[:, jw0:jw1],
                                         in_=ps1[:, jw0:jw1],
                                         func=AF.Exp, bias=ncexp[:]),
                            wait=[t_qa, t_nce] + tw_prev))
                        if it <= NT - 3:
                            texps.append(P.op("scalar", lambda E, it=it, jw1=jw1:
                                E.activation(out=probs[:, jw1:T],
                                             in_=ps1[:, jw1:T],
                                             func=AF.Exp, scale=1.0,
                                             bias=qlcols[:, it, 1:2]),
                                wait=[t_qa, t_qlcs[it]] + tw_prev))
                        t_texp[it] = texps[-1]
                        for g8 in range(2):
                            w_hb = []
                            if qi == 0 and half == 1:
                                w_hb = _after(t_outev.get(7))
                            if qi == 0 and half == 0:
                                w_hb = _after(t_probsT_free)

                            def gtp(E, g8=g8):
                                return [E.transpose(
                                    arena[:, 4 + g8].bitcast(BF16)[:, k8 * 128:(k8 + 1) * 128],
                                    probs[:, (g8 * 8 + k8) * 128:(g8 * 8 + k8 + 1) * 128],
                                    identb[:]) for k8 in range(8)]
                            ttp = P.op("tensor", gtp,
                                       wait=texps + [ti1] +
                                            _after(t_tp_ev.get((it - 1, g8)),
                                                   t_vt_ev if g8 == 1 else None) + w_hb)
                            t_tpmm[(it, g8)] = ttp
                            t_tpmm[it] = ttp
                            t_tp_ev[(it, g8)] = P.op(
                                "vector", lambda E, it=it, g8=g8, i_lo=i_lo: E.tensor_copy(
                                    bass_rust.AP(probsT.ap().tensor,
                                                 (g8 * 8) * 1024 + (it * 128 - i_lo),
                                                 [[NT * 1024, 128], [1024, 8], [1, 128]]),
                                    arena[:, 4 + g8].bitcast(BF16)[:, 0:1024]),
                                wait=[ttp])
                    t_trans_all = t_tp_ev[(half * 8 + 7, 1)]

                    if CSTG < 2:
                        t_tpass = t_trans_all
                        for qi in range(8):
                            t_outev[half * 8 + qi] = t_trans_all
                            t_srz_cp[half * 8 + qi] = t_trans_all
                        continue

                    t_tpass_last = None
                    for jt in range(NT):
                        hi = min(max((jt - 1) * 128 - i_lo, 0), 1024)
                        lo = min(max((jt + 2) * 128 - i_lo, 0), 1024)
                        if hi > 0:
                            t_tpass_last = P.op("vector", lambda E, jt=jt, hi=hi:
                                E.tensor_scalar(out=probsT[:, jt, 0:hi],
                                                in0=probsT[:, jt, 0:hi],
                                                scalar1=ewk[:, jt, 0:1], scalar2=None,
                                                op0=ALU.mult),
                                wait=[t_trans_all, t_ek[jt]])
                        if lo < 1024:
                            t_tpass_last = P.op("vector", lambda E, jt=jt, lo=lo:
                                E.tensor_scalar(out=probsT[:, jt, lo:1024],
                                                in0=probsT[:, jt, lo:1024],
                                                scalar1=ewk[:, jt, 62:63], scalar2=None,
                                                op0=ALU.mult),
                                wait=[t_trans_all, t_ek[jt]])
                        m_lo = max(max(jt - 1, 0) * 128, i_lo)
                        m_hi = min((jt + 2) * 128, i_lo + 1024)
                        if m_hi > m_lo:
                            J0 = jt * 128
                            Wit = m_hi - m_lo
                            ekst_ap = bass_rust.AP(
                                ekle_dram.ap().tensor,
                                J0 * (WPAD - 1) + 287 + m_lo,
                                [[WPAD - 1, 128], [1, Wit]])
                            t_erd = P.dma("sync", ekst[:, 0:Wit], ekst_ap,
                                          wait=[t_ekw[jt]] + _after(t_erd_last),
                                          chan="ekst")
                            t_tpass_last = P.op("vector",
                                lambda E, jt=jt, m_lo=m_lo, Wit=Wit, i_lo=i_lo:
                                E.tensor_tensor(
                                    out=probsT[:, jt, m_lo - i_lo:m_lo - i_lo + Wit],
                                    in0=probsT[:, jt, m_lo - i_lo:m_lo - i_lo + Wit],
                                    in1=ekst[:, 0:Wit], op=ALU.mult),
                                wait=[t_erd, t_trans_all])
                            t_erd_last = t_tpass_last
                            t_ekst_con[jt] = t_tpass_last
                    t_tpass = t_tpass_last

                    if CSTG < 3:
                        for qi in range(8):
                            t_outev[half * 8 + qi] = t_tpass
                            t_srz_cp[half * 8 + qi] = t_tpass
                        continue

                    for qi in range(8):
                        it = half * 8 + qi
                        I0 = it * 128
                        ic0 = I0 - i_lo
                        w0 = [t_tpass, t_vt[NT - 1]] + _after(t_outev.get(it - 1))
                        if it == 0:
                            w0 += _after(t_outh_free)

                        def gav(E, it=it, ic0=ic0):
                            return [E.matmul(psA, (vbb if jt >= it + 2 else vab)[:, jt, :],
                                             probsT[:, jt, ic0:ic0 + 128],
                                             start=(jt == 0), stop=False)
                                    for jt in range(NT)]
                        t_av = P.op("tensor", gav, wait=w0)
                        t_last_mm = t_av
                        nmr = [r for r in (0, 1) if it + r < NT]
                        for idx, r in enumerate(nmr):
                            jt = it + r
                            tmsk = P.op("vector", lambda E, jt=jt, r=r, ic0=ic0:
                                E.tensor_tensor(out=mscr[:],
                                                in0=probsT[:, jt, ic0:ic0 + 128],
                                                in1=mrm[:, r, :], op=ALU.mult),
                                wait=[t_tpass, t_mask, t_last_mm])
                            t_last_mm = P.op("tensor",
                                lambda E, last=(idx == len(nmr) - 1): E.matmul(
                                    psA, m64[:], mscr[:], start=False, stop=last),
                                wait=[tmsk, tm64])
                        if CSTG < 3.6:
                            t_srz = P.op("vector", lambda E, I0=I0: E.tensor_copy(
                                srzbuf[64:66, I0:I0 + 128], arena[:, 6][64:66, 0:128]),
                                wait=[t_last_mm])
                            t_srz_cp[it] = t_srz
                            if CSTG >= 3.5:
                                jts = [jj for jj in (it - 1, it, it + 1) if 0 <= jj < NT]
                                t_btm = None
                                for jj, jt in enumerate(jts):
                                    col = (jt - (it - 1)) * 128
                                    t_btm = P.op("tensor",
                                        lambda E, jt=jt, col=col, ic0=ic0: E.transpose(
                                            arena[:, 5].bitcast(BF16)[:, col:col + 128],
                                            probsT[:, jt, ic0:ic0 + 128], identb[:]),
                                        wait=[t_tpass, ti1] +
                                             _after(t_slab_ev_prev if jj == 0 else None,
                                                    t_btm))
                                if it in (0, NT - 1):
                                    t_ms = P.op("gpsimd", lambda E: E.memset(slab[:], 0.0),
                                                wait=_after(t_sw_prev))
                                else:
                                    t_ms = None
                                t_slab_ev = P.op("vector", lambda E, it=it: E.tensor_copy(
                                    slab[:, (0 if it > 0 else 128):
                                         (384 if it < NT - 1 else 256)],
                                    arena[:, 5].bitcast(BF16)[:, (0 if it > 0 else 128):
                                                              (384 if it < NT - 1 else 256)]),
                                    wait=[t_btm] + _after(t_ms, t_sw_prev))
                                t_slab_ev_prev = t_slab_ev
                                t_sw = P.dma("sync", slab_dram.ap()[:, 0:384],
                                             slab[:, 0:384],
                                             wait=[t_slab_ev] + _after(t_dread_last),
                                             chan="slabw")
                                t_sw_prev = t_sw
                                d_ap = bass_rust.AP(slab_dram.ap().tensor, 98,
                                                    [[393, 128], [1, 61]])
                                t_dr = P.dma("sync", dsl[:], d_ap, wait=[t_sw],
                                             chan="dread")
                                t_dread_last = t_dr
                                t_dt = P.op("tensor", lambda E: E.transpose(
                                    arena[:, 4].bitcast(BF16)[0:61, 0:128], dsl[:],
                                    identb[:]), wait=[t_dr, ti1] + _after(t_dtb_prev))
                                t_dtb = P.op("vector", lambda E: E.tensor_copy(
                                    dtb[:], arena[:, 4].bitcast(BF16)[0:61, 0:128]),
                                    wait=[t_dt])
                                t_dtb_prev = t_dtb
                            t_outev[it] = P.op("vector", lambda E, I0=I0:
                                E.tensor_copy(outh[:, I0:I0 + 128],
                                              arena[:, 6][0:64, 0:128]),
                                wait=[t_srz])
                            t_tb_last = t_outev[it]
                            continue
                        t_srz = P.op("vector", lambda E, I0=I0: E.tensor_copy(
                            srzbuf[64:66, I0:I0 + 128], arena[:, 6][64:66, 0:128]),
                            wait=[t_last_mm])
                        t_srz_cp[it] = t_srz
                        jts = [jt for jt in (it - 1, it, it + 1) if 0 <= jt < NT]
                        t_btm = None
                        for jj, jt in enumerate(jts):
                            col = (jt - (it - 1)) * 128
                            t_btm2 = P.op("tensor", lambda E, jt=jt, col=col, ic0=ic0:
                                E.transpose(
                                    arena[:, 5].bitcast(BF16)[:, col:col + 128],
                                    probsT[:, jt, ic0:ic0 + 128], identb[:]),
                                wait=[t_tpass, ti1] +
                                     _after(t_slab_ev_prev if jj == 0 else None,
                                            t_btm))
                            t_btm = t_btm2
                        if it in (0, NT - 1):
                            t_ms = P.op("gpsimd", lambda E: E.memset(slab[:], 0.0),
                                        wait=_after(t_sw_prev))
                        else:
                            t_ms = None
                        t_slab_ev = P.op("vector", lambda E, it=it: E.tensor_copy(
                            slab[:, (0 if it > 0 else 128):
                                 (384 if it < NT - 1 else 256)],
                            arena[:, 5].bitcast(BF16)[:, (0 if it > 0 else 128):
                                                      (384 if it < NT - 1 else 256)]),
                            wait=[t_btm] + _after(t_ms, t_sw_prev))
                        t_slab_ev_prev = t_slab_ev
                        t_sw = P.dma("sync", slab_dram.ap()[:, 0:384], slab[:, 0:384],
                                     wait=[t_slab_ev] + _after(t_dread_last),
                                     chan="slabw")
                        t_sw_prev = t_sw
                        d_ap = bass_rust.AP(slab_dram.ap().tensor, 98,
                                            [[393, 128], [1, 61]])
                        t_dr = P.dma("sync", dsl[:], d_ap, wait=[t_sw], chan="dread")
                        t_dread_last = t_dr
                        t_dt = P.op("tensor", lambda E: E.transpose(
                            arena[:, 4].bitcast(BF16)[0:61, 0:128], dsl[:],
                            identb[:]), wait=[t_dr, ti1] + _after(t_dtb_prev))
                        t_dtb = P.op("vector", lambda E: E.tensor_copy(
                            dtb[:], arena[:, 4].bitcast(BF16)[0:61, 0:128]),
                            wait=[t_dt])
                        t_dtb_prev = t_dtb
                        t_bs_mm = P.op("tensor", lambda E: E.matmul(
                            psBs[0:1, :], oc61[:], dtb[:], start=True, stop=True),
                            wait=[t_dtb, t_o61] + _after(t_tb_last))
                        t_bs_cp = P.op("vector", lambda E: E.tensor_copy(
                            bs[:], psBs[0:1, :]), wait=[t_bs_mm])
                        if CSTG < 3.7:
                            t_outev[it] = P.op("vector", lambda E, I0=I0:
                                E.tensor_copy(outh[:, I0:I0 + 128],
                                              arena[:, 6][0:64, 0:128]),
                                wait=[t_srz, t_bs_cp])
                            t_tb_last = t_outev[it]
                            continue
                        t_mv1 = P.op("tensor", lambda E, I0=I0: E.matmul(
                            arena[:, 7][:, 256:384][0:1, 0:128], selp[64:66, 0:1],
                            srzbuf[64:66, I0:I0 + 128], start=True, stop=True),
                            wait=[t_srz, t_selp, t_bs_cp])
                        t_mv1c = P.op("vector", lambda E: E.tensor_copy(
                            srow[:], arena[:, 7][:, 256:384][0:1, 0:128]),
                            wait=[t_mv1])
                        t_mv2 = P.op("tensor", lambda E, I0=I0: E.matmul(
                            arena[:, 7][:, 384:512][0:1, 0:128], selp[64:66, 1:2],
                            srzbuf[64:66, I0:I0 + 128], start=True, stop=True),
                            wait=[t_srz, t_selp, t_bs_cp, t_mv1c])
                        t_mv2c = P.op("vector", lambda E, I0=I0: E.tensor_copy(
                            zq[:], arena[:, 7][:, 384:512][0:1, 0:128]),
                            wait=[t_mv2])
                        t_zrowc = P.op("vector", lambda E, I0=I0: E.tensor_copy(
                            zrow[0:1, I0:I0 + 128], zq[:]), wait=[t_mv2c])
                        t_sl1 = P.op("vector", lambda E: E.tensor_tensor(
                            out=slrow[:], in0=zq[:], in1=srow[:], op=ALU.subtract),
                            wait=[t_mv1c, t_mv2c])
                        t_sl2 = P.op("vector", lambda E: E.tensor_tensor(
                            out=slrow[:], in0=slrow[:], in1=bs[:], op=ALU.subtract),
                            wait=[t_sl1, t_bs_cp])
                        t_slb = P.op("vector", lambda E: E.tensor_copy(
                            slrowb[:], slrow[:]), wait=[t_sl2])
                        t_srb = P.op("vector", lambda E: E.tensor_copy(
                            srowb[:], srow[:]), wait=[t_mv1c, t_slb])
                        if CSTG < 3.9:
                            t_outev[it] = P.op("vector", lambda E, I0=I0:
                                E.tensor_copy(outh[:, I0:I0 + 128],
                                              arena[:, 6][0:64, 0:128]),
                                wait=[t_srz, t_srb])
                            t_tb_last = t_outev[it]
                            continue
                        def gb(E):
                            return [E.matmul(psB[0:64, :], rv0f[:], slrowb[:],
                                             start=True, stop=False),
                                    E.matmul(psB[0:64, :], rv62f[:], srowb[:],
                                             start=False, stop=False),
                                    E.matmul(psB[0:64, :], rvband[:], dtb[:],
                                             start=False, stop=True)]
                        t_b = P.op("tensor", gb,
                                   wait=[t_slb, t_srb, t_rv0, t_rv62, t_rv, t_dtb,
                                         t_mv2c])
                        t_tb = P.op("vector", lambda E: E.tensor_copy(
                            tb64[:], psB[0:64, :]), wait=[t_b])
                        t_tb_last = t_tb
                        t_outev[it] = P.op("vector", lambda E, I0=I0: E.tensor_tensor(
                            out=outh[:, I0:I0 + 128], in0=arena[:, 6][0:64, 0:128],
                            in1=tb64[:], op=ALU.add), wait=[t_tb, t_srz])

                if CSTG < 3.7:
                    t_zr = P.op("vector", lambda E: E.memset(zrow[:], 1.0),
                                wait=[t_outev[NT - 1]])
                else:
                    t_zr = P.op("vector", lambda E: E.reciprocal(out=zrow[:], in_=zrow[:]),
                                wait=[t_outev[NT - 1]])
                t_sc_out = None
                for ncc in range(NC):
                    t_zb = P.op("tensor", lambda E, ncc=ncc: E.matmul(
                        arena[:, 4][0:64, :], onesf[:],
                        zrow[0:1, ncc * CN:(ncc + 1) * CN], start=True, stop=True),
                        wait=[t_zr, t_onesf] + _after(t_sc_out,
                             (t_dtb if CSTG >= 4 else None) if ncc == 0 else None))
                    t_zc = P.op("vector", lambda E: E.tensor_copy(
                        zbc[:], arena[:, 4][0:64, :]), wait=[t_zb])
                    t_sc_out = P.op("vector", lambda E, ncc=ncc: E.tensor_tensor(
                        out=outh[:, ncc * CN:(ncc + 1) * CN],
                        in0=outh[:, ncc * CN:(ncc + 1) * CN],
                        in1=zbc[:], op=ALU.mult),
                        wait=[t_zc, t_outev[NT - 1]])
                t_ow = P.dma("sync", xout_dram.ap()[h * 64:(h + 1) * 64, :], outh[:],
                             wait=[t_sc_out] + _after(t_outw_rd if h == 0 else None),
                             chan="outw")
                t_outh_free = t_ow
                t_qk_free = t_sc_last
                t_v_free = t_vt_mm
                t_rq_free = t_km_last
                t_rk_free = t_qm_last
                t_rv_free = t_outev[NT - 1] if CSTG < 3.9 else t_b
                t_probsT_free = t_outev[NT - 1]
                t_qa_last = t_qse_add[NT - 1]
                t_dtb_prev = t_zc
                if h == NH - 1:
                    t_qkv_rd_last = P.chan_last("vh")
            t_outw_all = P.chan_last("outw")

            for hc in range(HC):
                P.dma("gpsimd", xn[:, hc, :],
                      bass_rust.AP(xout_dram.ap().tensor, hc * 128 * T,
                                   [[T, 128], [1, T]]),
                      wait=[t_outw_all] + _after(t_norm_all if hc == 0 else None),
                      chan=f"px{b}")
            t_px_all = P.chan_last(f"px{b}")
            t_outw_rd = t_px_all
            yv = probsT[:].rearrange("p a b -> p (a b)")
            t_pev = {}
            t_pmm_last = None
            for oc in range(HC):
                for ncc in range(NC):
                    g = oc * NC + ncc
                    bank = g % 4

                    def gp(E, oc=oc, ncc=ncc, bank=bank):
                        return [E.matmul(arena[:, bank],
                                         wpb[:, hc2, oc * 128:(oc + 1) * 128],
                                         xn[:, hc2, ncc * CN:(ncc + 1) * CN],
                                         start=(hc2 == 0), stop=(hc2 == HC - 1))
                                for hc2 in range(HC)]
                    t_pm = P.op("tensor", gp,
                                wait=[t_px_all, t_wp_all] + _after(t_pev.get(g - 4)) +
                                     (_after(t_tpass, t_probsT_free) if g == 0 else []))
                    t_pmm_last = t_pm
                    t_pev[g] = P.op("scalar", lambda E, oc=oc, ncc=ncc, bank=bank:
                        E.activation(
                            out=yv[:, oc * T + ncc * CN:oc * T + ncc * CN + CN],
                            in_=arena[:, bank], func=AF.Identity,
                            bias=bcol[:, oc:oc + 1]), wait=[t_pm, t_bp])
            y_chunks = lambda hc: yv[:, hc * T:(hc + 1) * T]
            wait_y = {hc: t_pev[hc * NC + NC - 1] for hc in range(HC)}
            t_mm2 = stats_mms(y_chunks, wait_y,
                              wait_all=[t_pev[HC * NC - 1]])
            t_post_g2 = t_mm2[1]
            tg2, th2 = ln_basic(t_mm2)
            t_xr_prev = None
            t_st = {}
            for hc in range(HC):
                t_xr = P.dma("sync", xres[:], xT[b, hc], wait=_after(t_xr_prev),
                             chan="xres")
                t1 = P.op("vector", lambda E, hc=hc: E.tensor_tensor(
                    out=xoutb[:], in0=y_chunks(hc), in1=sv[:], op=ALU.mult),
                    wait=[tg2, th2, t_st.get(hc - 1)])
                t2 = P.op("vector", lambda E: E.tensor_tensor(
                    out=xoutb[:], in0=xoutb[:], in1=sm[:], op=ALU.subtract), wait=[t1])
                t3 = P.op("vector", lambda E: E.tensor_tensor(
                    out=xoutb[:], in0=xoutb[:], in1=xres[:], op=ALU.add),
                    wait=[t2, t_xr])
                t_xr_prev = t3
                t_st[hc] = P.dma("sync", xo[b, hc], xoutb[:], wait=[t3], chan="st")
                t_xn_free[hc] = t_pmm_last
            t_res_prev = t_xr_prev
            t_probsT_free = t1
            t_out_all.extend(t_st.values())

        P.finalize(t_out_all)
    return nc


def _stage_c(inputs, core, x2_full):
    b0 = core * B_PER_CORE
    xT = np.ascontiguousarray(
        x2_full[b0:b0 + B_PER_CORE].transpose(0, 2, 1)).reshape(B_PER_CORE, HC, 128, T)
    wq = inputs["L1_attn_w_qkv"]
    wqs = np.ascontiguousarray(
        wq.T.reshape(8, 128, 24, 128).transpose(2, 0, 1, 3))
    wp = inputs["L1_attn_w_proj"]
    wps = np.ascontiguousarray(wp.T).reshape(HC, 128, 1024)
    return {
        "xT": xT, "wqs": wqs, "wps": wps,
        "bpv": np.ascontiguousarray(inputs["L1_attn_b_proj"].reshape(HC, 128).T),
        "rpeq": inputs["L1_attn_rpe_q_table"],
        "rpek": inputs["L1_attn_rpe_k_table"],
        "rpev": inputs["L1_attn_rpe_v_table"],
    }


def run_c(inputs, x2_full):
    nc = build_launch_c()
    in_maps = [_stage_c(inputs, c, x2_full) for c in range(8)]
    res = run_bass_kernel_spmd(nc, in_maps, core_ids=list(range(8)))
    return _unstage_x(res.results)



def _stage_a(inputs, core):
    x = inputs["L0_attn_x"]
    b0 = core * B_PER_CORE
    xT = np.ascontiguousarray(
        x[b0:b0 + B_PER_CORE].transpose(0, 2, 1)).reshape(B_PER_CORE, HC, 128, T)
    vecs = np.stack([
        inputs["L0_attn_time_decay"].reshape(HC, 128).T,
        inputs["L0_attn_time_mix_v"].reshape(HC, 128).T,
        inputs["L0_attn_time_mix_r"].reshape(HC, 128).T,
    ], axis=1)
    return {
        "xT": xT,
        "WvT": np.ascontiguousarray(inputs["L0_attn_Wv"].T).reshape(HC, 128, 1024),
        "WrT": np.ascontiguousarray(inputs["L0_attn_Wr"].T).reshape(HC, 128, 1024),
        "WoT": np.ascontiguousarray(inputs["L0_attn_Wo"].T).reshape(HC, 128, 1024),
        "vecs": np.ascontiguousarray(vecs),
    }


def _unstage_x(res_list, key="xo"):
    outs = []
    for r in res_list:
        a = np.asarray(r[key], dtype=np.float32)
        outs.append(a.reshape(B_PER_CORE, 1024, T).transpose(0, 2, 1))
    return np.concatenate(outs, axis=0)


def run_a(inputs):
    nc = build_launch_a()
    in_maps = [_stage_a(inputs, c) for c in range(8)]
    res = run_bass_kernel_spmd(nc, in_maps, core_ids=list(range(8)))
    return _unstage_x(res.results)


def kernel(**inputs):
    x1 = run_a(inputs)
    x2 = run_b(inputs, x1)
    x3 = run_c(inputs, x2)
    x4 = run_d(inputs, x3)
    return x4.astype(np.float32)
